# version 04 (final)
# baseline (speedup 1.0000x reference)
import numpy as np
import ml_dtypes
import concourse.bass as bass
import concourse.mybir as mybir
from concourse.bass import IndirectOffsetOnAxis
from concourse.bass_utils import run_bass_kernel_spmd
from contextlib import ExitStack

F32 = mybir.dt.float32
BF16 = mybir.dt.bfloat16
U32 = mybir.dt.uint32
AF = mybir.ActivationFunctionType
ALU = mybir.AluOpType
AX = mybir.AxisListType

NCORES = 8
B_PER = 4
S = 2048
C = 1024
H = 16
NB = S // 128
NTOK_LOC = B_PER * S
E = 8
K_SEL = 2048
DFF = 4096
EXT = 48
GRP = 384
NOPS = GRP // 128
BIS_IT = 45
ATT_SCALE = C ** -0.5
LN_EPS = 1e-5

DEBUG = False


class Plan:
    def __init__(self):
        self.ops = {k: [] for k in ("gpsimd", "sync", "tensor", "vector", "scalar")}
        self.ctr = {}

    def cnt(self, sem):
        return self.ctr.get(id(sem), 0)

    def emit(self, eng, fn, waits=(), inc=None):
        if inc is not None:
            sem, amt = inc
            self.ctr[id(sem)] = self.ctr.get(id(sem), 0) + amt
        self.ops[eng].append((fn, tuple(waits), inc))

    def run(self, eng_name, eng):
        for fn, waits, inc in self.ops[eng_name]:
            for sem, val in waits:
                if val > 0:
                    eng.wait_ge(sem, val)
            inst = fn(eng)
            if inc is not None and inst is not None:
                inst.then_inc(inc[0], inc[1])


def attach(plan, block):
    @block.gpsimd
    def _(e):
        plan.run("gpsimd", e)

    @block.vector
    def _(e):
        plan.run("vector", e)

    @block.scalar
    def _(e):
        plan.run("scalar", e)

    @block.tensor
    def _(e):
        plan.run("tensor", e)

    @block.sync
    def _(e):
        plan.run("sync", e)


def build_fused(upto=5):
    nc = bass.Bass("TRN2", target_bir_lowering=False, debug=False)
    xin = nc.dram_tensor("xin", [B_PER, S, C], F32, kind="ExternalInput").ap()
    wq = nc.dram_tensor("wq", [C, C], F32, kind="ExternalInput").ap()
    wk = nc.dram_tensor("wk", [C, C], F32, kind="ExternalInput").ap()
    bq_r = nc.dram_tensor("bq_r", [128, 8], F32, kind="ExternalInput").ap()
    bk_r = nc.dram_tensor("bk_r", [128, 8], F32, kind="ExternalInput").ap()
    rw = nc.dram_tensor("rw", [C, E], F32, kind="ExternalInput").ap()
    rb_r = nc.dram_tensor("rb_r", [E, 1], F32, kind="ExternalInput").ap()
    idn_in = nc.dram_tensor("idn_in", [128, 128], F32, kind="ExternalInput").ap()
    ut_in = nc.dram_tensor("ut_in", [128, 128], F32, kind="ExternalInput").ap()
    gut_in = nc.dram_tensor("gut_in", [128, 128], F32, kind="ExternalInput").ap()
    gind_in = nc.dram_tensor("gind_in", [128, E], F32, kind="ExternalInput").ap()
    bigm_in = nc.dram_tensor("bigm_in", [128, E], F32, kind="ExternalInput").ap()
    cstA_in = nc.dram_tensor("cstA_in", [128, 2], F32, kind="ExternalInput").ap()
    cstB_in = nc.dram_tensor("cstB_in", [128, 8], F32, kind="ExternalInput").ap()
    offs_sc_in = nc.dram_tensor("offs_sc", [128, 1], U32, kind="ExternalInput").ap()
    offs_blk_in = nc.dram_tensor("offs_blk", [128, 1], U32, kind="ExternalInput").ap()
    w1_in = nc.dram_tensor("w1_in", [E, C, DFF], F32, kind="ExternalInput").ap()
    b1_r = nc.dram_tensor("b1_r", [128, 32 * E], F32, kind="ExternalInput").ap()
    w2_in = nc.dram_tensor("w2_in", [E, DFF, C], F32, kind="ExternalInput").ap()
    b2_r = nc.dram_tensor("b2_r", [E, C], F32, kind="ExternalInput").ap()

    yout = nc.dram_tensor("y", [NTOK_LOC, C], F32, kind="ExternalOutput").ap()

    w1b = nc.dram_tensor("w1b", [E, C, DFF], BF16)
    w2b = nc.dram_tensor("w2b", [E, DFF, C], BF16)
    b2b = nc.dram_tensor("b2b", [E, C], BF16)
    ybuf = nc.dram_tensor("ybuf", [NTOK_LOC + GRP, C], F32)
    x1bbuf = nc.dram_tensor("x1bbuf", [NTOK_LOC, C], BF16)
    scin = nc.dram_tensor("scin", [E, NTOK_LOC], F32)
    scall = nc.dram_tensor("scall", [NCORES * E, NTOK_LOC], F32)
    selbuf = nc.dram_tensor("selbuf", [128, 3 * EXT], F32)
    selall = nc.dram_tensor("selall", [NCORES * 128, 3 * EXT], F32)
    extbuf2 = nc.dram_tensor("extbuf2", [128 * EXT + GRP, 2], F32)
    if DEBUG:
        dbg_sc = nc.dram_tensor("dbg_sc", [128, 512], F32, kind="ExternalOutput").ap()
        dbg_sel = nc.dram_tensor("dbg_sel", [128, 3 * EXT], F32,
                                 kind="ExternalOutput").ap()
        dbg_cand = nc.dram_tensor("dbg_cand", [128, 3 * EXT], F32,
                                  kind="ExternalOutput").ap()
        dbg_rec = nc.dram_tensor("dbg_rec", [128, 3 * 24], F32,
                                 kind="ExternalOutput").ap()

    P = Plan()
    st = ExitStack()
    En = st.enter_context

    idn = En(nc.sbuf_tensor("idn", [128, 128], BF16))
    idn32 = En(nc.sbuf_tensor("idn32", [128, 128], F32))
    ut_sb = En(nc.sbuf_tensor("ut_sb", [128, 128], F32))
    gut_sb = En(nc.sbuf_tensor("gut_sb", [128, 128], F32))
    gind_sb = En(nc.sbuf_tensor("gind_sb", [128, E], F32))
    bigm_sb = En(nc.sbuf_tensor("bigm_sb", [128, E], F32))
    cstA = En(nc.sbuf_tensor("cstA", [128, 2], F32))
    cstB = En(nc.sbuf_tensor("cstB", [128, 8], F32))
    offs_sc = En(nc.sbuf_tensor("offs_sc_sb", [128, 1], U32))
    offs_blk = En(nc.sbuf_tensor("offs_blk_sb", [128, 1], U32))
    rw_sb = En(nc.sbuf_tensor("rw_sb", [128, 8 * E], BF16))
    bq_sb = En(nc.sbuf_tensor("bq_sb", [128, 8], F32))
    bk_sb = En(nc.sbuf_tensor("bk_sb", [128, 8], F32))
    rb_sb = En(nc.sbuf_tensor("rb_sb", [E, 1], F32))
    b1_sb = En(nc.sbuf_tensor("b1_sb", [128, 32 * E], F32))
    ones8 = En(nc.sbuf_tensor("ones8", [8, 1], F32))
    ones1_8 = En(nc.sbuf_tensor("ones1_8", [1, 8], F32))
    ones128 = En(nc.sbuf_tensor("ones128", [128, 1], F32))
    ones1_128 = En(nc.sbuf_tensor("ones1_128", [1, 128], F32))
    onesc = En(nc.sbuf_tensor("onesc", [1, 128], BF16))
    eps_t = En(nc.sbuf_tensor("eps_t", [128, 1], F32))
    mean_t = En(nc.sbuf_tensor("mean_t", [128, 1], F32))
    ss_t = En(nc.sbuf_tensor("ss_t", [128, 1], F32))
    lv_t = En(nc.sbuf_tensor("lv_t", [128, 1], F32))
    rstd_t = En(nc.sbuf_tensor("rstd_t", [128, 1], F32))
    sqj = En(nc.sbuf_tensor("sqj", [128, C], F32))

    dC = En(nc.semaphore("dC"))
    dX = [En(nc.semaphore(f"dX{i}")) for i in range(2)]
    dWk = [En(nc.semaphore(f"dWk{i}")) for i in range(2)]
    dWq = [En(nc.semaphore(f"dWq{i}")) for i in range(2)]
    dXr = [En(nc.semaphore(f"dXr{i}")) for i in range(2)]
    oYb = [En(nc.semaphore(f"oYb{i}")) for i in range(2)]
    oX1b = [En(nc.semaphore(f"oX1b{i}")) for i in range(2)]
    oSc = En(nc.semaphore("oSc"))
    cag1 = En(nc.semaphore("cag1"))
    cag2 = En(nc.semaphore("cag2"))
    dGsc = En(nc.semaphore("dGsc"))
    dBlk = En(nc.semaphore("dBlk"))
    oExt = En(nc.semaphore("oExt"))
    oEx2 = En(nc.semaphore("oEx2"))
    dRec = En(nc.semaphore("dRec"))
    dGx = [En(nc.semaphore(f"dGx{i}")) for i in range(2)]
    dCv = En(nc.semaphore("dCv"))
    dW10 = En(nc.semaphore("dW10"))
    dW1 = [En(nc.semaphore(f"dW1_{i}")) for i in range(2)]
    dW2 = [En(nc.semaphore(f"dW2_{i}")) for i in range(2)]
    dB2 = [En(nc.semaphore(f"dB2_{i}")) for i in range(2)]
    dSc = [En(nc.semaphore(f"dSc{i}")) for i in range(2)]
    dYz = En(nc.semaphore("dYz"))
    dZ = [En(nc.semaphore(f"dZ{i}")) for i in range(2)]
    oY = [En(nc.semaphore(f"oY{i}")) for i in range(2)]
    oDbg = En(nc.semaphore("oDbg"))
    sT = En(nc.semaphore("sT"))
    sV = En(nc.semaphore("sV"))
    sA = En(nc.semaphore("sA"))
    block = En(nc.Block())

    def g(fn, sem, waits=()):
        P.emit("gpsimd", fn, waits, (sem, 16))

    def so(fn, sem, waits=()):
        P.emit("sync", fn, waits, (sem, 16))

    def ad(fn, sem, waits=()):
        P.emit("scalar", fn, waits, (sem, 16))

    def vd(fn, sem, waits=()):
        P.emit("vector", fn, waits, (sem, 16))

    def t(fn, waits=(), inc=True):
        P.emit("tensor", fn, waits, (sT, 1) if inc else None)

    def v(fn, waits=()):
        P.emit("vector", fn, waits, (sV, 1))

    def a(fn, waits=()):
        P.emit("scalar", fn, waits, (sA, 1))

    g(lambda e: e.dma_start(out=rw_sb[:].rearrange("p (cc d) -> p cc d", cc=8),
                            in_=rw.rearrange("(cc p) d -> p cc d", p=128)), dC)
    g(lambda e: e.dma_start(out=bq_sb[:], in_=bq_r[:, :]), dC)
    g(lambda e: e.dma_start(out=bk_sb[:], in_=bk_r[:, :]), dC)
    g(lambda e: e.dma_start(out=rb_sb[:], in_=rb_r[:, :]), dC)
    g(lambda e: e.dma_start(out=idn[:], in_=idn_in[:, :]), dC)
    g(lambda e: e.dma_start(out=idn32[:], in_=idn_in[:, :]), dC)
    g(lambda e: e.dma_start(out=ut_sb[:], in_=ut_in[:, :]), dC)
    g(lambda e: e.dma_start(out=gut_sb[:], in_=gut_in[:, :]), dC)
    g(lambda e: e.dma_start(out=gind_sb[:], in_=gind_in[:, :]), dC)
    g(lambda e: e.dma_start(out=bigm_sb[:], in_=bigm_in[:, :]), dC)
    g(lambda e: e.dma_start(out=cstA[:], in_=cstA_in[:, :]), dC)
    g(lambda e: e.dma_start(out=cstB[:], in_=cstB_in[:, :]), dC)
    g(lambda e: e.dma_start(out=offs_sc[:], in_=offs_sc_in[:, :]), dC)
    g(lambda e: e.dma_start(out=offs_blk[:], in_=offs_blk_in[:, :]), dC)
    g(lambda e: e.dma_start(out=b1_sb[:], in_=b1_r[:, :]), dC)
    CONST = P.cnt(dC)

    v(lambda e: e.memset(ones8[:], 1.0))
    v(lambda e: e.memset(ones1_8[:], 1.0))
    v(lambda e: e.memset(ones128[:], 1.0))
    v(lambda e: e.memset(ones1_128[:], 1.0))
    v(lambda e: e.memset(onesc[:], 1.0))
    v(lambda e: e.memset(eps_t[:], LN_EPS))

    stA = ExitStack()
    EnA = stA.enter_context
    xT = EnA(nc.sbuf_tensor("xT", [128, 8 * 2048], BF16))
    kT = EnA(nc.sbuf_tensor("kT", [128, 8 * 2048], BF16))
    v_sb = EnA(nc.sbuf_tensor("v_sb", [128, NB * H * 65], BF16))
    qTs = EnA(nc.sbuf_tensor("qTs", [128, 512], BF16))
    P_sb = EnA(nc.sbuf_tensor("P_sb", [128, NB * 512], BF16))
    att = EnA(nc.sbuf_tensor("att", [128, 4 * C], F32))
    x1T = EnA(nc.sbuf_tensor("x1T", [128, 8 * 512], BF16))
    wqs = [EnA(nc.sbuf_tensor(f"wqs{i}", [128, 1024], BF16)) for i in range(2)]
    wks = [EnA(nc.sbuf_tensor(f"wks{i}", [128, 1024], BF16)) for i in range(2)]
    xf32 = [EnA(nc.sbuf_tensor(f"xf32_{i}", [128, C], F32)) for i in range(2)]
    xbf = [EnA(nc.sbuf_tensor(f"xbf_{i}", [128, C], BF16)) for i in range(2)]
    xres = [EnA(nc.sbuf_tensor(f"xres_{i}", [128, C], F32)) for i in range(2)]
    x1t = [EnA(nc.sbuf_tensor(f"x1t{i}", [128, C], F32)) for i in range(2)]
    x1bt = [EnA(nc.sbuf_tensor(f"x1bt{i}", [128, C], BF16)) for i in range(2)]
    u_sb = EnA(nc.sbuf_tensor("u_sb", [65, 512], F32))
    rs_sb = EnA(nc.sbuf_tensor("rs_sb", [128, 4], F32))
    p8 = EnA(nc.sbuf_tensor("p8", [E, 512], F32))
    rS8 = EnA(nc.sbuf_tensor("rS8", [1, 512], F32))
    scn = EnA(nc.sbuf_tensor("scn", [E, 512], F32))

    ps_tr = EnA(nc.psum_tensor("ps_tr", [128, 1024], BF16))
    ps_pj = EnA(nc.psum_tensor("ps_pj", [128, 512], F32))
    ps_L = EnA(nc.psum_tensor("ps_L", [128, 2048], F32))
    ps_AVU = EnA(nc.psum_tensor("ps_AVU", [128, 512], F32))
    ps_r = EnA(nc.psum_tensor("ps_r", [8, 512], F32))

    v(lambda e: e.memset(
        v_sb[:].rearrange("p (m h y) -> p m h y", m=NB, h=H)[:, :, :, 64:65], 1.0))
    ONES_V = P.cnt(sV)

    conv_jobs = []

    def _cw1(e_ix, i):
        conv_jobs.append(lambda e, e_ix=e_ix, i=i: e.dma_start(
            out=w1b[e_ix, i * 128:(i + 1) * 128, :],
            in_=w1_in[e_ix, i * 128:(i + 1) * 128, :]))

    def _cw2(e_ix, i):
        conv_jobs.append(lambda e, e_ix=e_ix, i=i: e.dma_start(
            out=w2b[e_ix, i * 512:(i + 1) * 512, :],
            in_=w2_in[e_ix, i * 512:(i + 1) * 512, :]))

    w1done = {}
    w2done = {}
    for i in range(8):
        _cw1(0, i)
    w1done[0] = len(conv_jobs)
    conv_jobs.append(lambda e: e.dma_start(out=b2b[:, :], in_=b2_r[:, :]))
    b2done = len(conv_jobs)
    for i in range(8):
        _cw2(0, i)
    w2done[0] = len(conv_jobs)
    for e_ix in range(1, E):
        for i in range(8):
            _cw1(e_ix, i)
        w1done[e_ix] = len(conv_jobs)
        for i in range(8):
            _cw2(e_ix, i)
        w2done[e_ix] = len(conv_jobs)
    if upto < 1:
        conv_jobs = []
    conv_state = [0]

    def drip():
        if conv_state[0] < len(conv_jobs):
            g(conv_jobs[conv_state[0]], dCv)
            conv_state[0] += 1

    prev_scn_v = [0]

    for b in range(B_PER):
        for tb in range(NB):
            r = tb % 2
            g(lambda e, b=b, tb=tb, r=r: e.dma_start(
                out=xf32[r][:], in_=xin[b, tb * 128:(tb + 1) * 128, :]),
              dX[r], waits=[(sV, P.cnt(sV))])
            ld_at = P.cnt(dX[r])
            v(lambda e, r=r: e.tensor_copy(xbf[r][:], xf32[r][:]),
              waits=[(dX[r], ld_at), (sT, P.cnt(sT))])
            cast_at = P.cnt(sV)
            for k in range(8):
                t(lambda e, r=r, k=k: e.transpose(
                    ps_tr[:, k * 128:(k + 1) * 128], xbf[r][:, k * 128:(k + 1) * 128],
                    idn[:, :]),
                  waits=([(sV, cast_at), (dC, CONST)] if k == 0 else []),
                  inc=(k == 7))
            tr_at = P.cnt(sT)
            v(lambda e, tb=tb: e.tensor_copy(
                xT[:].rearrange("p (cc n) -> p cc n", cc=8)[:, :, tb * 128:tb * 128 + 128],
                ps_tr[:].rearrange("p (cc k) -> p cc k", cc=8)),
              waits=[(sT, tr_at)])
        xT_v = P.cnt(sV)

        for dblk in range(8):
            r = dblk % 2
            g(lambda e, dblk=dblk, r=r: e.dma_start(
                out=wks[r][:].rearrange("p (cc d) -> p cc d", cc=8),
                in_=wk[:, dblk * 128:(dblk + 1) * 128].rearrange(
                    "(cc p) d -> p cc d", p=128)),
              dWk[r], waits=[(sT, P.cnt(sT))] if dblk >= 2 else [])
            wk_at = P.cnt(dWk[r])
            for ncn in range(4):
                evac_prior = P.cnt(sV)
                for cc in range(8):
                    t(lambda e, r=r, ncn=ncn, cc=cc: e.matmul(
                        ps_pj[:, :],
                        wks[r][:, cc * 128:(cc + 1) * 128],
                        xT[:, cc * 2048 + ncn * 512: cc * 2048 + ncn * 512 + 512],
                        start=(cc == 0), stop=(cc == 7)),
                      waits=([(sV, max(xT_v, evac_prior)), (dWk[r], wk_at)]
                             if cc == 0 else []),
                      inc=(cc == 7))
                mm_at = P.cnt(sT)
                v(lambda e, dblk=dblk, ncn=ncn: e.tensor_scalar(
                    out=kT[:, dblk * 2048 + ncn * 512: dblk * 2048 + ncn * 512 + 512],
                    in0=ps_pj[:, :], scalar1=bk_sb[:, dblk:dblk + 1], scalar2=None,
                    op0=ALU.add),
                  waits=[(sT, mm_at)])
            drip()
        kT_v = P.cnt(sV)

        for mb in range(NB):
            evac_prior = P.cnt(sV)
            for j in range(8):
                t(lambda e, j=j, mb=mb: e.transpose(
                    ps_tr[:, j * 128:(j + 1) * 128],
                    kT[:, j * 2048 + mb * 128: j * 2048 + mb * 128 + 128],
                    idn[:, :]),
                  waits=([(sV, max(kT_v, evac_prior))] if j == 0 else []),
                  inc=(j == 7))
            tr_at = P.cnt(sT)
            v(lambda e, mb=mb: e.tensor_copy(
                v_sb[:, mb * H * 65:(mb + 1) * H * 65].rearrange(
                    "p (h y) -> p h y", h=H)[:, :, 0:64],
                ps_tr[:].rearrange("p (h y) -> p h y", h=H)),
              waits=[(sT, tr_at)])
        v_v = P.cnt(sV)

        for ncn in range(4):
            for hp in range(8):
                r = hp % 2
                g(lambda e, hp=hp, r=r: e.dma_start(
                    out=wqs[r][:].rearrange("p (cc d) -> p cc d", cc=8),
                    in_=wq[:, hp * 128:(hp + 1) * 128].rearrange(
                        "(cc p) d -> p cc d", p=128)),
                  dWq[r], waits=[(sT, P.cnt(sT))] if (b + ncn + hp) > 0 else [])
                wq_at = P.cnt(dWq[r])
                for cc in range(8):
                    t(lambda e, r=r, cc=cc, ncn=ncn: e.matmul(
                        ps_pj[:, :],
                        wqs[r][:, cc * 128:(cc + 1) * 128],
                        xT[:, cc * 2048 + ncn * 512: cc * 2048 + ncn * 512 + 512],
                        start=(cc == 0), stop=(cc == 7)),
                      waits=([(dWq[r], wq_at), (sV, P.cnt(sV))] if cc == 0 else []),
                      inc=(cc == 7))
                mm_at = P.cnt(sT)
                v(lambda e, hp=hp: e.tensor_scalar(
                    out=qTs[:, :], in0=ps_pj[:, :],
                    scalar1=bq_sb[:, hp:hp + 1], scalar2=None, op0=ALU.add),
                  waits=[(sT, mm_at)])
                qts_v = P.cnt(sV)

                for hh in range(2):
                    h = 2 * hp + hh
                    rb_ = hh * 64
                    for mg in range(4):
                        exp_prior = P.cnt(sA)
                        for mi in range(4):
                            mb = mg * 4 + mi
                            t(lambda e, rb_=rb_, hp=hp, mb=mb, mi=mi: e.matmul(
                                ps_L[:, mi * 512:(mi + 1) * 512],
                                kT[rb_:rb_ + 64,
                                   hp * 2048 + mb * 128: hp * 2048 + mb * 128 + 128],
                                qTs[rb_:rb_ + 64, :],
                                start=True, stop=True),
                              waits=([(sA, exp_prior), (sV, qts_v)] if mi == 0 else []),
                              inc=(mi == 3))
                        lg_at = P.cnt(sT)
                        a(lambda e, mg=mg: e.activation(
                            out=P_sb[:, mg * 2048:(mg + 1) * 2048],
                            in_=ps_L[:, :], func=AF.Exp, scale=ATT_SCALE),
                          waits=[(sT, lg_at)])
                    exp_at = P.cnt(sA)
                    dve_prior = P.cnt(sV)
                    for mb in range(NB):
                        t(lambda e, mb=mb, h=h: e.matmul(
                            ps_AVU[0:65, :],
                            v_sb[:, mb * H * 65 + h * 65: mb * H * 65 + h * 65 + 65],
                            P_sb[:, mb * 512:(mb + 1) * 512],
                            start=(mb == 0), stop=(mb == NB - 1)),
                          waits=([(sA, exp_at),
                                  (sV, max(v_v, dve_prior, ONES_V))] if mb == 0 else []),
                          inc=(mb == NB - 1))
                    av_at = P.cnt(sT)
                    v(lambda e: e.tensor_copy(u_sb[:, :], ps_AVU[0:65, :]),
                      waits=[(sT, av_at)])
                    u_v = P.cnt(sV)
                    for q in range(4):
                        t(lambda e, q=q: e.transpose(
                            ps_AVU[:, q * 65:(q + 1) * 65],
                            u_sb[0:65, q * 128:(q + 1) * 128],
                            idn32[0:65, 0:65]),
                          waits=([(sV, u_v)] if q == 0 else []),
                          inc=(q == 3))
                    ut_at = P.cnt(sT)
                    v(lambda e: e.reciprocal(
                        rs_sb[:, :],
                        ps_AVU[:, 0:260].rearrange("p (q y) -> p q y", q=4)[:, :, 64:65]),
                      waits=[(sT, ut_at)])
                    rs_v = P.cnt(sV)
                    v(lambda e, h=h: e.tensor_tensor(
                        out=att[:].rearrange("p (q c) -> p q c", q=4)[:, :, h * 64:h * 64 + 64],
                        in0=ps_AVU[:, 0:260].rearrange("p (q y) -> p q y", q=4)[:, :, 0:64],
                        in1=rs_sb[:, :].rearrange("p (q o) -> p q o", o=1).to_broadcast(
                            [128, 4, 64]),
                        op=ALU.mult),
                      waits=[(sV, rs_v)])
                drip()
            att_v = P.cnt(sV)

            for t4 in range(4):
                tb = ncn * 4 + t4
                r = t4 % 2
                g(lambda e, b=b, tb=tb, r=r: e.dma_start(
                    out=xres[r][:], in_=xin[b, tb * 128:(tb + 1) * 128, :]),
                  dXr[r], waits=[(sV, P.cnt(sV))])
                res_at = P.cnt(dXr[r])
                v(lambda e, r=r, t4=t4: e.tensor_tensor(
                    out=xres[r][:], in0=att[:, t4 * C:(t4 + 1) * C], in1=xres[r][:],
                    op=ALU.add),
                  waits=[(dXr[r], res_at), (sV, att_v)])
                v(lambda e, r=r: e.reduce_sum(mean_t[:, :], xres[r][:], axis=AX.X),
                  waits=[(sV, P.cnt(sV))])
                v(lambda e: e.tensor_scalar(out=mean_t[:, :], in0=mean_t[:, :],
                                            scalar1=1.0 / C, scalar2=None, op0=ALU.mult),
                  waits=[(sV, P.cnt(sV))])
                v(lambda e, r=r: e.tensor_scalar(out=xres[r][:], in0=xres[r][:],
                                                 scalar1=mean_t[:, 0:1], scalar2=None,
                                                 op0=ALU.subtract),
                  waits=[(sV, P.cnt(sV))])
                zc_v = P.cnt(sV)
                a(lambda e, r=r: e.activation(out=sqj[:, :], in_=xres[r][:],
                                              func=AF.Square, accum_out=ss_t[:, :]),
                  waits=[(sV, zc_v), (sA, P.cnt(sA))])
                a(lambda e: e.activation(out=lv_t[:, :], in_=ss_t[:, :],
                                         func=AF.Ln, scale=1.0 / C, bias=eps_t[:, 0:1]),
                  waits=[(sA, P.cnt(sA))])
                a(lambda e: e.activation(out=rstd_t[:, :], in_=lv_t[:, :],
                                         func=AF.Exp, scale=-0.5),
                  waits=[(sA, P.cnt(sA))])
                rstd_a = P.cnt(sA)
                v(lambda e, r=r: e.tensor_scalar(out=x1t[r][:], in0=xres[r][:],
                                                 scalar1=rstd_t[:, 0:1], scalar2=None,
                                                 op0=ALU.mult),
                  waits=[(sA, rstd_a), (sV, P.cnt(sV)), (oYb[r], P.cnt(oYb[r]))])
                x1_v = P.cnt(sV)
                v(lambda e, r=r: e.tensor_copy(x1bt[r][:], x1t[r][:]),
                  waits=[(sV, x1_v), (oX1b[r], P.cnt(oX1b[r]))])
                x1b_v = P.cnt(sV)
                so(lambda e, b=b, tb=tb, r=r: e.dma_start(
                    out=ybuf[b * S + tb * 128: b * S + (tb + 1) * 128, :],
                    in_=x1t[r][:]),
                   oYb[r], waits=[(sV, x1_v)])
                so(lambda e, b=b, tb=tb, r=r: e.dma_start(
                    out=x1bbuf[b * S + tb * 128: b * S + (tb + 1) * 128, :],
                    in_=x1bt[r][:]),
                   oX1b[r], waits=[(sV, x1b_v)])
                for k in range(8):
                    t(lambda e, r=r, k=k: e.transpose(
                        ps_tr[:, k * 128:(k + 1) * 128],
                        x1bt[r][:, k * 128:(k + 1) * 128], idn[:, :]),
                      waits=([(sV, x1b_v)] if k == 0 else []),
                      inc=(k == 7))
                xtr_at = P.cnt(sT)
                v(lambda e, t4=t4: e.tensor_copy(
                    x1T[:].rearrange("p (cc n) -> p cc n", cc=8)[:, :, t4 * 128:(t4 + 1) * 128],
                    ps_tr[:].rearrange("p (cc k) -> p cc k", cc=8)),
                  waits=[(sT, xtr_at)])
            x1T_v = P.cnt(sV)

            for cc in range(8):
                t(lambda e, cc=cc: e.matmul(
                    ps_r[:, :],
                    rw_sb[:, cc * E:(cc + 1) * E],
                    x1T[:, cc * 512:(cc + 1) * 512],
                    start=(cc == 0), stop=(cc == 7)),
                  waits=([(sV, x1T_v), (sA, P.cnt(sA))] if cc == 0 else []),
                  inc=(cc == 7))
            rmm_at = P.cnt(sT)
            a(lambda e: e.activation(out=p8[:, :], in_=ps_r[:, :],
                                     func=AF.Exp, bias=rb_sb[:, 0:1]),
              waits=[(sT, rmm_at), (sA, P.cnt(sA)), (sV, prev_scn_v[0])])
            p8_a = P.cnt(sA)
            t(lambda e: e.matmul(ps_r[0:1, :], ones8[:, :], p8[:, :],
                                 start=True, stop=True),
              waits=[(sA, p8_a), (sV, 6)])
            smm_at = P.cnt(sT)
            v(lambda e: e.reciprocal(rS8[:, :], ps_r[0:1, :]),
              waits=[(sT, smm_at)])
            rs8_v = P.cnt(sV)
            t(lambda e: e.matmul(ps_r[:, :], ones1_8[:, :], rS8[:, :],
                                 start=True, stop=True),
              waits=[(sV, rs8_v)])
            bc_at = P.cnt(sT)
            v(lambda e: e.tensor_tensor(out=scn[:, :], in0=p8[:, :], in1=ps_r[:, :],
                                        op=ALU.mult),
              waits=[(sT, bc_at), (oSc, P.cnt(oSc))])
            scn_v = P.cnt(sV)
            prev_scn_v[0] = scn_v
            so(lambda e, b=b, ncn=ncn: e.dma_start(
                out=scin[:, b * S + ncn * 512: b * S + (ncn + 1) * 512],
                in_=scn[:, :]),
               oSc, waits=[(sV, scn_v)])

    ATT_T = P.cnt(sT)
    ATT_V = P.cnt(sV)
    ATT_A = P.cnt(sA)
    SC_ST = P.cnt(oSc)
    YB_ST = [P.cnt(oYb[0]), P.cnt(oYb[1])]
    XB_ST = [P.cnt(oX1b[0]), P.cnt(oX1b[1])]
    stA.close()

    stB = ExitStack()
    EnB = stB.enter_context
    sc = EnB(nc.sbuf_tensor("sc_sb", [128, 512], F32))
    work = EnB(nc.sbuf_tensor("work", [128, 512], F32))
    vals = EnB(nc.sbuf_tensor("vals", [128, EXT], F32))
    colu = EnB(nc.sbuf_tensor("colu", [128, EXT], U32))
    colf = EnB(nc.sbuf_tensor("colf", [128, EXT], F32))
    tok = EnB(nc.sbuf_tensor("tok", [128, EXT], F32))
    lo_t = EnB(nc.sbuf_tensor("lo_t", [128, 1], F32))
    hi_t = EnB(nc.sbuf_tensor("hi_t", [128, 1], F32))
    mid_t = EnB(nc.sbuf_tensor("mid_t", [128, 1], F32))
    cntp = EnB(nc.sbuf_tensor("cntp", [128, 1], F32))
    cbc = EnB(nc.sbuf_tensor("cbc", [128, 1], F32))
    selm = EnB(nc.sbuf_tensor("selm", [128, 1], U32))
    mgt = EnB(nc.sbuf_tensor("mgt", [128, EXT], F32))
    meq = EnB(nc.sbuf_tensor("meq", [128, EXT], F32))
    scr = EnB(nc.sbuf_tensor("scr", [128, EXT], F32))
    sel = EnB(nc.sbuf_tensor("sel", [128, EXT], F32))
    rkeq = EnB(nc.sbuf_tensor("rkeq", [128, EXT], F32))
    zer48 = EnB(nc.sbuf_tensor("zer48", [128, EXT], F32))
    cgt = EnB(nc.sbuf_tensor("cgt", [128, 1], F32))
    ceq = EnB(nc.sbuf_tensor("ceq", [128, 1], F32))
    beq = EnB(nc.sbuf_tensor("beq", [128, 1], F32))
    c1b = EnB(nc.sbuf_tensor("c1b", [128, 1], F32))
    qb = EnB(nc.sbuf_tensor("qb", [128, 1], F32))
    sbrow = EnB(nc.sbuf_tensor("sbrow", [1, 128], F32))
    ext3 = EnB(nc.sbuf_tensor("ext3", [128, 3 * EXT], F32))
    mycand = EnB(nc.sbuf_tensor("mycand", [128, 3 * EXT], F32))
    selcnt = EnB(nc.sbuf_tensor("selcnt", [128, 1], F32))
    cnt8sb = EnB(nc.sbuf_tensor("cnt8sb", [8, 1], F32))
    cntrow = EnB(nc.sbuf_tensor("cntrow", [1, 8], F32))
    cnt_bc = EnB(nc.sbuf_tensor("cnt_bc", [128, 8], F32))
    selbL = EnB(nc.sbuf_tensor("selbL", [128, 1], F32))
    selbm = EnB(nc.sbuf_tensor("selbm", [128, 1], F32))
    sbb = EnB(nc.sbuf_tensor("sbb", [128, 128], F32))
    cmpm = EnB(nc.sbuf_tensor("cmpm", [128, 128], F32))
    pcnt = EnB(nc.sbuf_tensor("pcnt", [128, 1], F32))
    bof = EnB(nc.sbuf_tensor("bof", [128, 1], F32))
    srcf = EnB(nc.sbuf_tensor("srcf", [128, 24], F32))
    srcu = EnB(nc.sbuf_tensor("srcu", [128, 24], U32))
    ext2 = EnB(nc.sbuf_tensor("ext2", [128, 2 * EXT], F32))
    cmp2 = EnB(nc.sbuf_tensor("cmp2", [128, 48], F32))
    idxf = EnB(nc.sbuf_tensor("idxf", [128, 24], F32))
    idgf = EnB(nc.sbuf_tensor("idgf", [128, 24], F32))
    valid1 = EnB(nc.sbuf_tensor("valid1", [128, 1], U32))
    idxu = EnB(nc.sbuf_tensor("idxu", [128, 24], U32))
    idgu = EnB(nc.sbuf_tensor("idgu", [128, 24], U32))
    toks = [EnB(nc.sbuf_tensor(f"toks{i}", [128, C], BF16)) for i in range(2)]
    toksT = [EnB(nc.sbuf_tensor(f"toksT{i}", [128, 8 * GRP], BF16)) for i in range(2)]
    h1T = EnB(nc.sbuf_tensor("h1T", [128, 32 * GRP], BF16))
    w10 = EnB(nc.sbuf_tensor("w10", [128, 16 * 1024], BF16))
    w1s = [EnB(nc.sbuf_tensor(f"w1s{i}", [128, 1024], BF16)) for i in range(2)]
    w2s = [EnB(nc.sbuf_tensor(f"w2s{i}", [128, 1024], BF16)) for i in range(2)]
    b2row = [EnB(nc.sbuf_tensor(f"b2row{i}", [1, C], BF16)) for i in range(2)]
    contrib = [EnB(nc.sbuf_tensor(f"contrib{i}", [128, C], F32)) for i in range(2)]
    z_t = [EnB(nc.sbuf_tensor(f"z_t{i}", [128, C], F32)) for i in range(2)]
    o_t = [EnB(nc.sbuf_tensor(f"o_t{i}", [128, C], F32)) for i in range(2)]

    psb = EnB(nc.psum_tensor("psb", [128, 512], F32))
    ps_tx = EnB(nc.psum_tensor("ps_tx", [128, 1024], BF16))
    ps_eo = [EnB(nc.psum_tensor(f"ps_eo{i}", [128, 1024], F32)) for i in range(NOPS)]

    while conv_state[0] < len(conv_jobs):
        drip()
    CVALL = 16 * len(conv_jobs)

    FIN0 = [(oSc, SC_ST), (oYb[0], YB_ST[0]), (oYb[1], YB_ST[1]),
            (oX1b[0], XB_ST[0]), (oX1b[1], XB_ST[1]), (dCv, CVALL)]
    if upto <= 1:
        P.emit("sync", lambda e: None, waits=FIN0, inc=None)
        attach(P, block)
        stB.close()
        st.close()
        return nc

    for fb in range(16):
        so(lambda e, fb=fb: e.dma_start(
            out=w10[:, fb * 1024:(fb + 1) * 1024].rearrange(
                "p (cc d) -> p cc d", cc=8),
            in_=w1b[0, :, fb * 128:(fb + 1) * 128].rearrange(
                "(cc p) d -> p cc d", p=128)),
           dW10, waits=[(dCv, CVALL)] if fb == 0 else [])

    P.emit("gpsimd", lambda e: e.collective_compute(
        "AllGather", ALU.bypass, replica_groups=[list(range(NCORES))],
        ins=[scin.ap().opt()], outs=[scall.ap().opt()]),
        waits=[(oSc, SC_ST)], inc=(cag1, 1))
    g(lambda e: e.indirect_dma_start(
        out=sc[:, :], out_offset=None,
        in_=scall.ap().rearrange("a (q t) -> (a q) t", t=512),
        in_offset=IndirectOffsetOnAxis(offs_sc[:, 0:1], axis=0)),
      dGsc, waits=[(cag1, 1)])
    GSC = P.cnt(dGsc)

    v(lambda e: e.tensor_copy(work[:], sc[:]),
      waits=[(dGsc, GSC), (sT, ATT_T), (sA, ATT_A)])
    wv = P.cnt(sV)
    for rnd in range(6):
        v(lambda e, rnd=rnd: e.max(out=vals[:, rnd * 8:(rnd + 1) * 8], in_=work[:]),
          waits=[(sV, wv)])
        mx_v = P.cnt(sV)
        v(lambda e, rnd=rnd: e.max_index(out=colu[:, rnd * 8:(rnd + 1) * 8],
                                         in_max=vals[:, rnd * 8:(rnd + 1) * 8],
                                         in_values=work[:]),
          waits=[(sV, mx_v)])
        mi_v = P.cnt(sV)
        v(lambda e, rnd=rnd: e.match_replace(out=work[:],
                                             in_to_replace=vals[:, rnd * 8:(rnd + 1) * 8],
                                             in_values=work[:], imm_value=-1e30),
          waits=[(sV, mi_v)])
        wv = P.cnt(sV)
    v(lambda e: e.tensor_copy(colf[:], colu[:]), waits=[(sV, wv)])
    v(lambda e: e.tensor_scalar(out=tok[:], in0=colf[:], scalar1=cstA[:, 0:1],
                                scalar2=None, op0=ALU.add),
      waits=[(sV, P.cnt(sV))])
    ext_v = P.cnt(sV)

    v(lambda e: e.memset(lo_t[:], 0.0))
    v(lambda e: e.memset(hi_t[:], 1.0))
    for it in range(BIS_IT):
        v(lambda e: e.tensor_scalar(out=mid_t[:], in0=lo_t[:], scalar1=hi_t[:, 0:1],
                                    scalar2=0.5, op0=ALU.add, op1=ALU.mult),
          waits=[(sV, P.cnt(sV))])
        mid_v = P.cnt(sV)
        v(lambda e: e.tensor_scalar(out=scr[:, 0:EXT], in0=vals[:], scalar1=mid_t[:, 0:1],
                                    scalar2=None, op0=ALU.is_ge, op1=ALU.add,
                                    accum_out=cntp[:, :]),
          waits=[(sV, mid_v)])
        cnt_v = P.cnt(sV)
        t(lambda e: e.matmul(psb[0:1, 0:1], cntp[:, :], ones128[:, :],
                             start=True, stop=True),
          waits=[(sV, cnt_v)])
        tm1 = P.cnt(sT)
        v(lambda e: e.tensor_copy(sbrow[0:1, 0:1], psb[0:1, 0:1]),
          waits=[(sT, tm1)])
        sb_v = P.cnt(sV)
        t(lambda e: e.matmul(psb[:, 1:2], ones1_128[:, :], sbrow[0:1, 0:1],
                             start=True, stop=True),
          waits=[(sV, sb_v)])
        tm2 = P.cnt(sT)
        v(lambda e: e.tensor_copy(cbc[:], psb[:, 1:2]), waits=[(sT, tm2)])
        cbc_v = P.cnt(sV)
        v(lambda e: e.tensor_scalar(out=selm[:], in0=cbc[:], scalar1=float(K_SEL),
                                    scalar2=None, op0=ALU.is_ge),
          waits=[(sV, cbc_v)])
        selm_v = P.cnt(sV)
        v(lambda e: e.copy_predicated(lo_t[:], selm[:], mid_t[:]),
          waits=[(sV, selm_v)])
        v(lambda e: e.tensor_scalar(out=selm[:], in0=cbc[:], scalar1=float(K_SEL),
                                    scalar2=None, op0=ALU.is_lt),
          waits=[(sV, P.cnt(sV))])
        v(lambda e: e.copy_predicated(hi_t[:], selm[:], mid_t[:]),
          waits=[(sV, P.cnt(sV))])
    tau_v = P.cnt(sV)

    v(lambda e: e.tensor_scalar(out=mgt[:], in0=vals[:], scalar1=lo_t[:, 0:1],
                                scalar2=None, op0=ALU.is_gt, op1=ALU.add,
                                accum_out=cgt[:, :]),
      waits=[(sV, tau_v)])
    v(lambda e: e.tensor_scalar(out=meq[:], in0=vals[:], scalar1=lo_t[:, 0:1],
                                scalar2=None, op0=ALU.is_equal, op1=ALU.add,
                                accum_out=ceq[:, :]),
      waits=[(sV, P.cnt(sV))])
    mk_v = P.cnt(sV)
    t(lambda e: e.matmul(psb[0:1, 0:1], cgt[:, :], ones128[:, :], start=True, stop=True),
      waits=[(sV, mk_v)])
    c1t = P.cnt(sT)
    v(lambda e: e.tensor_copy(sbrow[0:1, 0:1], psb[0:1, 0:1]), waits=[(sT, c1t)])
    c1s_v = P.cnt(sV)
    t(lambda e: e.matmul(psb[:, 1:2], ones1_128[:, :], sbrow[0:1, 0:1],
                         start=True, stop=True),
      waits=[(sV, c1s_v)])
    c1t2 = P.cnt(sT)
    v(lambda e: e.tensor_copy(c1b[:], psb[:, 1:2]), waits=[(sT, c1t2)])
    c1_v = P.cnt(sV)
    v(lambda e: e.tensor_scalar(out=qb[:], in0=c1b[:], scalar1=-1.0,
                                scalar2=float(K_SEL), op0=ALU.mult, op1=ALU.add),
      waits=[(sV, c1_v)])
    qb_v = P.cnt(sV)
    t(lambda e: e.matmul(psb[:, 2:3], ut_sb[:, :], ceq[:, :], start=True, stop=True),
      waits=[(sV, mk_v)])
    be_at = P.cnt(sT)
    v(lambda e: e.tensor_copy(beq[:], psb[:, 2:3]), waits=[(sT, be_at)])
    be_v = P.cnt(sV)
    v(lambda e: e.memset(zer48[:], 0.0))
    z48_v = P.cnt(sV)
    v(lambda e: e.tensor_tensor_scan(out=rkeq[:], data0=meq[:], data1=zer48[:],
                                     initial=0.0, op0=ALU.add, op1=ALU.add),
      waits=[(sV, max(mk_v, z48_v))])
    v(lambda e: e.tensor_tensor(out=rkeq[:], in0=rkeq[:], in1=meq[:], op=ALU.subtract),
      waits=[(sV, P.cnt(sV))])
    v(lambda e: e.tensor_scalar(out=rkeq[:], in0=rkeq[:], scalar1=beq[:, 0:1],
                                scalar2=None, op0=ALU.add),
      waits=[(sV, P.cnt(sV)), (sV, be_v)])
    rk_v = P.cnt(sV)
    v(lambda e: e.tensor_scalar(out=scr[:], in0=rkeq[:], scalar1=qb[:, 0:1],
                                scalar2=None, op0=ALU.is_lt),
      waits=[(sV, rk_v), (sV, qb_v)])
    v(lambda e: e.tensor_tensor(out=scr[:], in0=scr[:], in1=meq[:], op=ALU.mult),
      waits=[(sV, P.cnt(sV))])
    v(lambda e: e.tensor_tensor(out=sel[:], in0=scr[:], in1=mgt[:], op=ALU.max),
      waits=[(sV, P.cnt(sV))])
    sel_v = P.cnt(sV)

    v(lambda e: e.tensor_copy(ext3[:, 0:EXT], tok[:]), waits=[(sV, sel_v)])
    v(lambda e: e.tensor_copy(ext3[:, EXT:2 * EXT], vals[:]), waits=[(sV, P.cnt(sV))])
    v(lambda e: e.tensor_copy(ext3[:, 2 * EXT:3 * EXT], sel[:]),
      waits=[(sV, P.cnt(sV))])
    ext3_v = P.cnt(sV)
    so(lambda e: e.dma_start(out=selbuf[:, :], in_=ext3[:]),
       oExt, waits=[(sV, ext3_v)])
    if upto <= 2:
        P.emit("sync", lambda e: None,
               waits=FIN0 + [(oExt, P.cnt(oExt))], inc=None)
        attach(P, block)
        stB.close()
        st.close()
        return nc
    P.emit("gpsimd", lambda e: e.collective_compute(
        "AllGather", ALU.bypass, replica_groups=[list(range(NCORES))],
        ins=[selbuf.ap().opt()], outs=[selall.ap().opt()]),
        waits=[(oExt, P.cnt(oExt))], inc=(cag2, 1))

    g(lambda e: e.indirect_dma_start(
        out=mycand[:, :], out_offset=None,
        in_=selall.ap(),
        in_offset=IndirectOffsetOnAxis(offs_blk[:, 0:1], axis=0)),
      dBlk, waits=[(cag2, 1)])
    BLK = P.cnt(dBlk)
    if DEBUG:
        so(lambda e: e.dma_start(out=dbg_sc[:, :], in_=sc[:]), oDbg,
           waits=[(sV, P.cnt(sV))])
        so(lambda e: e.dma_start(out=dbg_sel[:, :], in_=ext3[:]), oDbg,
           waits=[(sV, P.cnt(sV))])

    v(lambda e: e.memset(z_t[0][:], 0.0))
    ztz_v = P.cnt(sV)
    for kk in range(NOPS):
        so(lambda e, kk=kk: e.dma_start(
            out=ybuf[NTOK_LOC + kk * 128: NTOK_LOC + (kk + 1) * 128, :],
            in_=z_t[0][:]),
           dYz, waits=[(sV, ztz_v)] if kk == 0 else [])

    v(lambda e: e.reduce_sum(selcnt[:, :], mycand[:, 2 * EXT:3 * EXT], axis=AX.X),
      waits=[(dBlk, BLK), (sV, P.cnt(sV))])
    scnt_v = P.cnt(sV)
    t(lambda e: e.matmul(psb[0:8, 4:5], gind_sb[:, :], selcnt[:, :],
                         start=True, stop=True),
      waits=[(sV, scnt_v)])
    c8_at = P.cnt(sT)
    v(lambda e: e.tensor_copy(cnt8sb[:, :], psb[0:8, 4:5]), waits=[(sT, c8_at)])
    c8_v = P.cnt(sV)
    t(lambda e: e.transpose(psb[0:1, 8:16], cnt8sb[:, :], idn32[0:8, 0:8]),
      waits=[(sV, c8_v)])
    cr_at = P.cnt(sT)
    v(lambda e: e.tensor_copy(cntrow[:, :], psb[0:1, 8:16]), waits=[(sT, cr_at)])
    cr_v = P.cnt(sV)
    t(lambda e: e.matmul(psb[:, 16:24], ones1_128[:, :], cntrow[:, :],
                         start=True, stop=True),
      waits=[(sV, cr_v)])
    cbc_at = P.cnt(sT)
    v(lambda e: e.tensor_copy(cnt_bc[:, :], psb[:, 16:24]), waits=[(sT, cbc_at)])
    cnt_bc_v = P.cnt(sV)
    t(lambda e: e.matmul(psb[:, 24:25], gut_sb[:, :], selcnt[:, :],
                         start=True, stop=True),
      waits=[(sV, scnt_v)])
    sbl_at = P.cnt(sT)
    v(lambda e: e.tensor_copy(selbL[:, :], psb[:, 24:25]), waits=[(sT, sbl_at)])
    selbL_v = P.cnt(sV)

    for e_ix in range(E):
        v(lambda e, e_ix=e_ix: e.tensor_scalar(
            out=selbm[:, :], in0=selbL[:, :], scalar1=bigm_sb[:, e_ix:e_ix + 1],
            scalar2=None, op0=ALU.add),
          waits=[(sV, max(selbL_v, P.cnt(sV)))])
        sbm_v = P.cnt(sV)
        t(lambda e: e.transpose(psb[0:1, 32:160], selbm[:, :], idn32[:, :]),
          waits=[(sV, sbm_v)])
        tr_at = P.cnt(sT)
        v(lambda e: e.tensor_copy(sbrow[:, :], psb[0:1, 32:160]), waits=[(sT, tr_at)])
        sbr_v = P.cnt(sV)
        t(lambda e: e.matmul(psb[:, 160:288], ones1_128[:, :], sbrow[:, :],
                             start=True, stop=True),
          waits=[(sV, sbr_v)])
        sbb_at = P.cnt(sT)
        v(lambda e: e.tensor_copy(sbb[:, :], psb[:, 160:288]), waits=[(sT, sbb_at)])
        sbb_v = P.cnt(sV)
        for kk in range(NOPS):
            k3 = e_ix * NOPS + kk
            v(lambda e, kk=kk: e.tensor_tensor(
                out=cmpm[:, :], in0=sbb[:, :],
                in1=cstB[:, kk:kk + 1].to_broadcast([128, 128]),
                op=ALU.is_le),
              waits=[(sV, P.cnt(sV))])
            v(lambda e: e.reduce_sum(pcnt[:, :], cmpm[:, :], axis=AX.X),
              waits=[(sV, P.cnt(sV))])
            v(lambda e: e.tensor_tensor(out=cmpm[:, :], in0=sbb[:, :], in1=cmpm[:, :],
                                        op=ALU.mult),
              waits=[(sV, P.cnt(sV))])
            v(lambda e: e.reduce_max(bof[:, :], cmpm[:, :], axis=AX.X),
              waits=[(sV, P.cnt(sV))])
            v(lambda e, k3=k3, e_ix=e_ix: e.tensor_scalar(
                out=srcf[:, k3:k3 + 1], in0=pcnt[:, :],
                scalar1=float(EXT), scalar2=float(16 * EXT * e_ix - EXT),
                op0=ALU.mult, op1=ALU.add),
              waits=[(sV, P.cnt(sV))])
            v(lambda e, k3=k3, kk=kk: e.tensor_scalar(
                out=srcf[:, k3:k3 + 1], in0=srcf[:, k3:k3 + 1],
                scalar1=cstB[:, kk:kk + 1], scalar2=None, op0=ALU.add),
              waits=[(sV, P.cnt(sV))])
            v(lambda e, k3=k3: e.tensor_scalar(
                out=srcf[:, k3:k3 + 1], in0=srcf[:, k3:k3 + 1],
                scalar1=bof[:, 0:1], scalar2=None, op0=ALU.subtract),
              waits=[(sV, P.cnt(sV))])
    v(lambda e: e.tensor_copy(srcu[:, :], srcf[:, :]), waits=[(sV, P.cnt(sV))])
    src_v = P.cnt(sV)

    v(lambda e: e.tensor_copy(
        ext2[:].rearrange("p (j w) -> p j w", w=2)[:, :, 0:1],
        mycand[:, 0:EXT].rearrange("p (j o) -> p j o", o=1)),
      waits=[(dBlk, BLK), (sV, P.cnt(sV))])
    v(lambda e: e.tensor_copy(
        ext2[:].rearrange("p (j w) -> p j w", w=2)[:, :, 1:2],
        mycand[:, EXT:2 * EXT].rearrange("p (j o) -> p j o", o=1)),
      waits=[(sV, P.cnt(sV))])
    ext2_v = P.cnt(sV)
    so(lambda e: e.dma_start(
        out=extbuf2.ap()[0:128 * EXT, :].rearrange("(p j) w -> p (j w)", p=128),
        in_=ext2[:]),
       oEx2, waits=[(sV, ext2_v)])
    so(lambda e: e.dma_start(
        out=extbuf2.ap()[128 * EXT:128 * EXT + GRP, :].rearrange(
            "(p j) w -> p (j w)", p=128),
        in_=zer48[:, 0:2 * GRP // 128]),
       oEx2, waits=[(sV, z48_v)])
    EX2 = P.cnt(oEx2)
    if upto <= 3:
        P.emit("sync", lambda e: None,
               waits=FIN0 + [(oEx2, EX2), (dBlk, BLK), (sV, src_v)], inc=None)
        attach(P, block)
        stB.close()
        st.close()
        return nc

    relu_prior = [0]
    eo_evac_v = [0, 0, 0]
    tposed_after = [0, 0]
    toksT_vs = [0] * E
    relu_dones = [0] * E
    eo_ats = [0] * E

    def emit_gather(e_ix):
        for kk in range(NOPS):
            k3 = e_ix * NOPS + kk
            g(lambda e, k3=k3: e.indirect_dma_start(
                out=cmp2[:, 2 * k3:2 * k3 + 2], out_offset=None,
                in_=extbuf2.ap(),
                in_offset=IndirectOffsetOnAxis(srcu[:, k3:k3 + 1], axis=0)),
              dRec, waits=([(sV, src_v), (oEx2, EX2)] if k3 == 0 else []))
        rec_at = P.cnt(dRec)
        for kk in range(NOPS):
            k3 = e_ix * NOPS + kk
            v(lambda e, k3=k3: e.tensor_scalar(
                out=idxf[:, k3:k3 + 1], in0=cmp2[:, 2 * k3:2 * k3 + 1],
                scalar1=cstB[:, 6:7], scalar2=None, op0=ALU.subtract),
              waits=[(dRec, rec_at), (sV, P.cnt(sV))])
            v(lambda e, kk=kk, e_ix=e_ix: e.tensor_scalar(
                out=valid1[:, :], in0=cstB[:, kk:kk + 1],
                scalar1=cnt_bc[:, e_ix:e_ix + 1], scalar2=None, op0=ALU.is_ge),
              waits=[(sV, max(P.cnt(sV), cnt_bc_v))])
            v(lambda e, k3=k3: e.tensor_copy(idgf[:, k3:k3 + 1], idxf[:, k3:k3 + 1]),
              waits=[(sV, P.cnt(sV))])
            v(lambda e, k3=k3: e.copy_predicated(
                idgf[:, k3:k3 + 1], valid1[:, :], cstB[:, 3:4]),
              waits=[(sV, P.cnt(sV))])
            v(lambda e, k3=k3: e.copy_predicated(
                idxf[:, k3:k3 + 1], valid1[:, :], cstB[:, 7:8]),
              waits=[(sV, P.cnt(sV))])
        v(lambda e, e_ix=e_ix: e.tensor_copy(
            idxu[:, e_ix * NOPS:(e_ix + 1) * NOPS],
            idxf[:, e_ix * NOPS:(e_ix + 1) * NOPS]),
          waits=[(sV, P.cnt(sV))])
        v(lambda e, e_ix=e_ix: e.tensor_copy(
            idgu[:, e_ix * NOPS:(e_ix + 1) * NOPS],
            idgf[:, e_ix * NOPS:(e_ix + 1) * NOPS]),
          waits=[(sV, P.cnt(sV))])
        idx_v = P.cnt(sV)
        if DEBUG and e_ix == E - 1:
            so(lambda e: e.dma_start(out=dbg_cand[:, :], in_=mycand[:]), oDbg,
               waits=[(sV, P.cnt(sV))])
            so(lambda e: e.dma_start(out=dbg_rec[:, 0:48], in_=cmp2[:]), oDbg,
               waits=[(sV, idx_v)])
            so(lambda e: e.dma_start(out=dbg_rec[:, 48:72], in_=idxf[:]), oDbg,
               waits=[(sV, idx_v)])

        par = e_ix % 2
        for kk in range(NOPS):
            k3 = e_ix * NOPS + kk
            r = kk % 2
            w = [(sV, idx_v), (sT, tposed_after[r])]
            if e_ix == 0 and kk == 0:
                w += [(oX1b[0], XB_ST[0]), (oX1b[1], XB_ST[1])]
            g(lambda e, k3=k3, r=r: e.indirect_dma_start(
                out=toks[r][:, :], out_offset=None,
                in_=x1bbuf.ap(),
                in_offset=IndirectOffsetOnAxis(idgu[:, k3:k3 + 1], axis=0)),
              dGx[r], waits=w)
            gk = P.cnt(dGx[r])
            evac_prior = P.cnt(sV)
            for cc in range(8):
                t(lambda e, r=r, cc=cc: e.transpose(
                    ps_tx[:, cc * 128:(cc + 1) * 128], toks[r][:, cc * 128:(cc + 1) * 128],
                    idn[:, :]),
                  waits=([(dGx[r], gk), (sV, evac_prior)] if cc == 0 else []),
                  inc=(cc == 7))
            tr_at = P.cnt(sT)
            tposed_after[r] = tr_at
            v(lambda e, kk=kk, par=par: e.tensor_copy(
                toksT[par][:].rearrange("p (cc n) -> p cc n", cc=8)[:, :, kk * 128:(kk + 1) * 128],
                ps_tx[:].rearrange("p (cc k) -> p cc k", cc=8)),
              waits=[(sT, tr_at)])
        toksT_vs[e_ix] = P.cnt(sV)

    def emit_ffn(e_ix):
        par = e_ix % 2
        for fb in range(32):
            r = fb % 2
            if e_ix == 0 and fb < 16:
                w1_at = 16 * 16
            else:
                em1 = so if fb % 2 == 0 else ad
                wv1 = [(sT, P.cnt(sT))] if (e_ix, fb) > (0, 17) else []
                if e_ix == 0 and fb == 17:
                    wv1 = wv1 + [(dCv, CVALL)]
                em1(lambda e, fb=fb, r=r, e_ix=e_ix: e.dma_start(
                    out=w1s[r][:].rearrange("p (cc d) -> p cc d", cc=8),
                    in_=w1b[e_ix, :, fb * 128:(fb + 1) * 128].rearrange(
                        "(cc p) d -> p cc d", p=128)),
                    dW1[r], waits=wv1)
                w1_at = P.cnt(dW1[r])
            rp = P.cnt(sA)
            for cc in range(8):
                if e_ix == 0 and fb < 16:
                    src = w10[:, fb * 1024 + cc * 128: fb * 1024 + (cc + 1) * 128]
                    wl = [(dW10, w1_at)]
                else:
                    src = None
                    wl = [(dW1[r], w1_at)]
                t(lambda e, r=r, cc=cc, fb=fb, src=src: e.matmul(
                    psb[:, 0:GRP],
                    src if src is not None else w1s[r][:, cc * 128:(cc + 1) * 128],
                    toksT[par][:, cc * GRP:(cc + 1) * GRP],
                    start=(cc == 0), stop=(cc == 7)),
                  waits=([(sV, toksT_vs[e_ix]), (sA, rp)] + wl if cc == 0 else []),
                  inc=(cc == 7))
            h1_at = P.cnt(sT)
            a(lambda e, fb=fb, e_ix=e_ix: e.activation(
                out=h1T[:, fb * GRP:(fb + 1) * GRP], in_=psb[:, 0:GRP],
                func=AF.Relu, bias=b1_sb[:, e_ix * 32 + fb:e_ix * 32 + fb + 1]),
              waits=[(sT, h1_at), (sA, P.cnt(sA))])
        relu_done = P.cnt(sA)
        relu_dones[e_ix] = relu_done

        rb2 = e_ix % 2
        wb2 = [(sT, P.cnt(sT))] if e_ix >= 2 else []
        ad(lambda e, e_ix=e_ix, rb2=rb2: e.dma_start(
            out=b2row[rb2][:, :], in_=b2b[e_ix:e_ix + 1, :]),
           dB2[rb2], waits=wb2)
        b2_at = P.cnt(dB2[rb2])
        for fb2 in range(32):
            r = fb2 % 2
            emitter = ad if fb2 % 2 == 0 else so
            wv2 = [(sT, P.cnt(sT))] if (e_ix, fb2) > (0, 1) else []
            emitter(lambda e, fb2=fb2, r=r, e_ix=e_ix: e.dma_start(
                out=w2s[r][:, :],
                in_=w2b[e_ix, fb2 * 128:(fb2 + 1) * 128, :]),
                dW2[r], waits=wv2)
            w2_at = P.cnt(dW2[r])
            for sc3 in range(NOPS):
                for hf in range(2):
                    w = []
                    if hf == 0 and fb2 == 0:
                        w.append((sV, eo_evac_v[sc3]))
                    if hf == 0 and sc3 == 0:
                        w += [(sA, relu_done), (dW2[r], w2_at)]
                    t(lambda e, fb2=fb2, r=r, sc3=sc3, hf=hf: e.matmul(
                        ps_eo[sc3][:, hf * 512:(hf + 1) * 512],
                        h1T[:, fb2 * GRP + sc3 * 128: fb2 * GRP + sc3 * 128 + 128],
                        w2s[r][:, hf * 512:(hf + 1) * 512],
                        start=(fb2 == 0), stop=False),
                      waits=w, inc=(sc3 == NOPS - 1 and hf == 1))
        for sc3 in range(NOPS):
            for hf in range(2):
                t(lambda e, sc3=sc3, hf=hf, rb2=rb2: e.matmul(
                    ps_eo[sc3][:, hf * 512:(hf + 1) * 512],
                    onesc[:, :], b2row[rb2][:, hf * 512:(hf + 1) * 512],
                    start=False, stop=True),
                  waits=([(dB2[rb2], b2_at), (sV, 5)] if sc3 == 0 and hf == 0 else []),
                  inc=(sc3 == NOPS - 1 and hf == 1))
        eo_ats[e_ix] = P.cnt(sT)

    def emit_scatter(e_ix):
        for sc3 in range(NOPS):
            k3 = e_ix * NOPS + sc3
            rc = k3 % 2
            v(lambda e, rc=rc, k3=k3, sc3=sc3: e.tensor_scalar(
                out=contrib[rc][:, :], in0=ps_eo[sc3][:, :],
                scalar1=cmp2[:, 2 * k3 + 1:2 * k3 + 2], scalar2=None, op0=ALU.mult),
              waits=[(sT, eo_ats[e_ix]), (dSc[rc], P.cnt(dSc[rc]))])
            eo_evac_v[sc3] = P.cnt(sV)
            c_v = P.cnt(sV)
            if upto == 45:
                continue
            w = [(sV, c_v), (dSc[1 - rc], P.cnt(dSc[1 - rc]))]
            if e_ix == 0 and sc3 == 0:
                w += [(oYb[0], YB_ST[0]), (oYb[1], YB_ST[1]), (dYz, P.cnt(dYz))]
            g(lambda e, rc=rc, k3=k3: e.indirect_dma_start(
                out=ybuf.ap(),
                out_offset=IndirectOffsetOnAxis(idxu[:, k3:k3 + 1], axis=0),
                in_=contrib[rc][:, :], in_offset=None,
                bounds_check=NTOK_LOC + GRP - 1, oob_is_err=True,
                compute_op=ALU.add),
              dSc[rc], waits=w)

    if upto <= 4:
        for e_ix in range(E):
            emit_gather(e_ix)
        P.emit("sync", lambda e: None,
               waits=FIN0 + [(oEx2, EX2), (sV, P.cnt(sV)), (sT, P.cnt(sT)),
                             (dGx[0], P.cnt(dGx[0])), (dGx[1], P.cnt(dGx[1])),
                             (dRec, P.cnt(dRec))], inc=None)
        attach(P, block)
        stB.close()
        st.close()
        return nc

    for e_ix in range(E):
        emit_gather(e_ix)
        if e_ix >= 1:
            emit_ffn(e_ix - 1)
            emit_scatter(e_ix - 1)
    emit_ffn(E - 1)
    emit_scatter(E - 1)
    SCAT = [P.cnt(dSc[0]), P.cnt(dSc[1])]
    if upto in (45, 46):
        P.emit("sync", lambda e: None,
               waits=FIN0 + [(sV, P.cnt(sV)), (sT, P.cnt(sT)), (sA, P.cnt(sA)),
                             (dSc[0], SCAT[0]), (dSc[1], SCAT[1]),
                             (dGx[0], P.cnt(dGx[0])), (dGx[1], P.cnt(dGx[1]))],
               inc=None)
        attach(P, block)
        stB.close()
        st.close()
        return nc

    for tb in range(NTOK_LOC // 128):
        r = tb % 2
        so(lambda e, tb=tb, r=r: e.dma_start(
            out=z_t[r][:], in_=ybuf[tb * 128:(tb + 1) * 128, :]),
           dZ[r],
           waits=([(dSc[0], SCAT[0]), (dSc[1], SCAT[1])] if tb == 0
                  else [(sV, P.cnt(sV))]))
        ld_at = P.cnt(dZ[r])
        v(lambda e, r=r: e.reduce_sum(mean_t[:, :], z_t[r][:], axis=AX.X),
          waits=[(dZ[r], ld_at), (sV, P.cnt(sV))])
        v(lambda e: e.tensor_scalar(out=mean_t[:, :], in0=mean_t[:, :],
                                    scalar1=1.0 / C, scalar2=None, op0=ALU.mult),
          waits=[(sV, P.cnt(sV))])
        v(lambda e, r=r: e.tensor_scalar(out=z_t[r][:], in0=z_t[r][:],
                                         scalar1=mean_t[:, 0:1], scalar2=None,
                                         op0=ALU.subtract),
          waits=[(sV, P.cnt(sV))])
        zc_v = P.cnt(sV)
        a(lambda e, r=r: e.activation(out=sqj[:, :], in_=z_t[r][:], func=AF.Square,
                                      accum_out=ss_t[:, :]),
          waits=[(sV, zc_v), (sA, P.cnt(sA))])
        a(lambda e: e.activation(out=lv_t[:, :], in_=ss_t[:, :], func=AF.Ln,
                                 scale=1.0 / C, bias=eps_t[:, 0:1]),
          waits=[(sA, P.cnt(sA))])
        a(lambda e: e.activation(out=rstd_t[:, :], in_=lv_t[:, :], func=AF.Exp,
                                 scale=-0.5),
          waits=[(sA, P.cnt(sA))])
        rstd_a = P.cnt(sA)
        v(lambda e, r=r: e.tensor_scalar(out=o_t[r][:], in0=z_t[r][:],
                                         scalar1=rstd_t[:, 0:1], scalar2=None,
                                         op0=ALU.mult),
          waits=[(sA, rstd_a), (sV, P.cnt(sV)), (oY[r], P.cnt(oY[r]))])
        o_v = P.cnt(sV)
        so(lambda e, tb=tb, r=r: e.dma_start(
            out=yout[tb * 128:(tb + 1) * 128, :], in_=o_t[r][:]),
           oY[r], waits=[(sV, o_v)])

    fin = [(oY[0], P.cnt(oY[0])), (oY[1], P.cnt(oY[1]))]
    if DEBUG:
        fin.append((oDbg, P.cnt(oDbg)))
    P.emit("sync", lambda e: None, waits=fin, inc=None)
    attach(P, block)
    stB.close()
    st.close()
    return nc


_cache = {}


def _get(name, builder):
    if name not in _cache:
        _cache[name] = builder()
    return _cache[name]


def make_inputs(inputs):
    f32 = np.float32
    x = np.ascontiguousarray(np.asarray(inputs["L0_attn_x"], f32))
    Wq = np.ascontiguousarray(np.asarray(inputs["L0_attn_Wq"], f32))
    Wk = np.ascontiguousarray(np.asarray(inputs["L0_attn_Wk"], f32))
    bq = np.asarray(inputs["L0_attn_bq"], f32)
    bk = np.asarray(inputs["L0_attn_bk"], f32)
    rw = np.ascontiguousarray(np.asarray(inputs["L0_ffn_router_w"], f32))
    rb = np.asarray(inputs["L0_ffn_router_b"], f32)
    w1 = np.ascontiguousarray(np.asarray(inputs["L0_ffn_w1"], f32))
    b1 = np.asarray(inputs["L0_ffn_b1"], f32)
    w2 = np.ascontiguousarray(np.asarray(inputs["L0_ffn_w2"], f32))
    b2 = np.ascontiguousarray(np.asarray(inputs["L0_ffn_b2"], f32))

    idn = np.eye(128, dtype=f32)
    bq_r = bq.reshape(8, 128).T.copy()
    bk_r = bk.reshape(8, 128).T.copy()
    rb_r = rb.reshape(E, 1).copy()
    b1_r = np.ascontiguousarray(
        b1.reshape(E, 32, 128).transpose(2, 0, 1).reshape(128, E * 32))

    p_idx = np.arange(128, dtype=f32)
    ut = np.triu(np.ones((128, 128), dtype=f32), 1)
    gut = np.zeros((128, 128), dtype=f32)
    for gix in range(8):
        gut[gix * 16:(gix + 1) * 16, gix * 16:(gix + 1) * 16] = \
            np.triu(np.ones((16, 16), dtype=f32), 1)
    gind = np.zeros((128, E), dtype=f32)
    gind[np.arange(128), (np.arange(128) // 16)] = 1.0
    bigm = (1.0 - gind) * 1e9
    cstA = np.zeros((128, 2), dtype=f32)
    cstA[:, 0] = p_idx * 512
    cstA[:, 1] = 1.0

    in_maps = []
    for c in range(NCORES):
        cstB = np.zeros((128, 8), dtype=f32)
        for kk in range(NOPS):
            cstB[:, kk] = kk * 128 + p_idx
        cstB[:, 6] = float(c * NTOK_LOC)
        cstB[:, 7] = NTOK_LOC + p_idx
        p = np.arange(128)
        offs_sc = (((p // 16) * 8 + c) * 16 + (p % 16)).astype(np.uint32)
        offs_blk = ((p // 16) * 128 + 16 * c + (p % 16)).astype(np.uint32)
        in_maps.append({
            "xin": x[B_PER * c:B_PER * (c + 1)],
            "wq": Wq, "wk": Wk, "bq_r": bq_r, "bk_r": bk_r,
            "rw": rw, "rb_r": rb_r, "idn_in": idn,
            "ut_in": ut, "gut_in": gut, "gind_in": gind, "bigm_in": bigm,
            "cstA_in": cstA, "cstB_in": cstB,
            "offs_sc": offs_sc.reshape(128, 1),
            "offs_blk": offs_blk.reshape(128, 1),
            "w1_in": w1, "b1_r": b1_r, "w2_in": w2, "b2_r": b2,
        })
    return in_maps


def kernel(**inputs):
    nc = _get("fused", build_fused)
    in_maps = make_inputs(inputs)
    res = run_bass_kernel_spmd(nc, in_maps, core_ids=list(range(NCORES)))
    out = np.concatenate([res.results[c]["y"] for c in range(NCORES)], axis=0)
    return out.reshape(NCORES * B_PER, S, C).astype(np.float32)
